# version 02 (final)
# baseline (speedup 1.0000x reference)
import numpy as np
import concourse.bass as bass
import concourse.mybir as mybir
from concourse.bass_utils import run_bass_kernel_spmd

F32 = mybir.dt.float32
BF16 = mybir.dt.bfloat16
AF = mybir.ActivationFunctionType
ALU = mybir.AluOpType
AX = mybir.AxisListType

T = 4096
D = 1024
DC = 8
TB = 32
NCORE = 8


class Ser:

    def __init__(self, nc):
        self.nc = nc
        self.steps = []

    def add(self, eng, w, fn):
        self.steps.append((eng, w, fn))

    def t(self, fn): self.add("tensor", 1, fn)
    def v(self, fn): self.add("vector", 1, fn)
    def s(self, fn): self.add("scalar", 1, fn)
    def d(self, fn): self.add("sync", 16, fn)

    def build(self):
        nc = self.nc
        S = nc.alloc_semaphore("Sser")
        total = sum(w for (_, w, _) in self.steps)
        with nc.Block() as block:
            def mk(name):
                def body(eng):
                    cum = 0
                    for (en, w, fn) in self.steps:
                        if en == name:
                            eng.wait_ge(S, cum)
                            fn(eng).then_inc(S, w)
                        cum += w
                    if name == "sync":
                        eng.wait_ge(S, total)
                return body
            block.sync(mk("sync"))
            block.tensor(mk("tensor"))
            block.vector(mk("vector"))
            block.scalar(mk("scalar"))


class P:

    def __init__(self, name):
        self.nc = bass.Bass("TRN2", target_bir_lowering=False)
        self.ser = Ser(self.nc)
        self.name = name
        self._ceps = self.nc.alloc_sbuf_tensor("ceps", [128, 1], F32)
        self.nc.const_aps.register(F32, 1e-5, self._ceps[:, 0:1])
        self.MEMSET(self._ceps[:, 0:1], 1e-5)

    def din(self, name, shape):
        return self.nc.dram_tensor(name, list(shape), F32, kind="ExternalInput").ap()

    def dout(self, name, shape):
        return self.nc.dram_tensor(name, list(shape), F32, kind="ExternalOutput").ap()

    def dtmp(self, name, shape, dt=F32):
        return self.nc.dram_tensor(name, list(shape), dt, kind="Internal").ap()

    def sb(self, name, shape, dt=F32):
        return self.nc.alloc_sbuf_tensor(name, list(shape), dt)

    def ps(self, name, shape):
        return self.nc.alloc_psum_tensor(name, list(shape), F32)

    def DMA(self, dst, src):
        self.ser.d(lambda e, a=dst, b=src: e.dma_start(out=a, in_=b))

    def MM(self, out, lhsT, rhs, start=True, stop=True):
        self.ser.t(lambda e, o=out, l=lhsT, r=rhs, s1=start, s2=stop:
                   e.matmul(o, l, r, start=s1, stop=s2))

    def TR(self, out, in_, iden):
        self.ser.t(lambda e, o=out, i=in_, d=iden: e.transpose(o, i, d))

    def ACT(self, out, in_, func, bias=0.0, scale=1.0, accum=None):
        if accum is None:
            self.ser.s(lambda e, o=out, i=in_, f=func, b=bias, sc=scale:
                       e.activation(o, i, f, bias=b, scale=sc))
        else:
            self.ser.s(lambda e, o=out, i=in_, f=func, b=bias, sc=scale, a=accum:
                       e.activation(o, i, f, bias=b, scale=sc, accum_out=a))

    def SMUL(self, out, in_, c):
        self.ser.s(lambda e, o=out, i=in_, cc=c: e.mul(o, i, cc))

    def CPY(self, out, in_):
        self.ser.v(lambda e, o=out, i=in_: e.tensor_copy(o, i))

    def TT(self, op, out, a, b):
        self.ser.v(lambda e, o=out, x=a, y=b, p=op: e.tensor_tensor(o, x, y, p))

    def TSS(self, out, in_, scalar, op):
        self.ser.v(lambda e, o=out, i=in_, s=scalar, p=op:
                   e.tensor_single_scalar(o, i, s, p))

    def TS2(self, out, in_, s1, s2, op0, op1):
        self.ser.v(lambda e, o=out, i=in_, a=s1, b=s2, p0=op0, p1=op1:
                   e.tensor_scalar(o, i, a, b, p0, p1))

    def RED(self, out, in_, op=ALU.add, absval=False):
        self.ser.v(lambda e, o=out, i=in_, p=op, av=absval:
                   e.tensor_reduce(o, i, AX.X, p, apply_absolute_value=av))

    def RECIP(self, out, in_):
        self.ser.v(lambda e, o=out, i=in_: e.reciprocal(o, i))

    def MEMSET(self, ap, c):
        self.ser.v(lambda e, a=ap, cc=c: e.memset(a, cc))

    def STT(self, out, in0, scalar, in1, op0, op1):
        self.ser.v(lambda e, o=out, i0=in0, s=scalar, i1=in1, p0=op0, p1=op1:
                   e.scalar_tensor_tensor(o, i0, s, i1, p0, p1))

    def SCAN(self, out, d0, d1, init, op0, op1):
        self.ser.v(lambda e, o=out, a=d0, b=d1, i=init, p0=op0, p1=op1:
                   e.tensor_tensor_scan(o, a, b, i, p0, p1))

    def ln_plain(self, h, xt, scr, m, ss, rstd):
        self.RED(m, xt, ALU.add)
        self.SMUL(m, m, 1.0 / D)
        self.TSS(h, xt, m, ALU.subtract)
        self.ACT(scr, h, AF.Square, accum=ss)
        self.ACT(rstd, ss, AF.Sqrt, bias=1e-5, scale=1.0 / D)
        self.RECIP(rstd, rstd)
        self.TSS(h, h, rstd, ALU.mult)

    def bcast_row(self, dst_sb, row_ap, ones_1x128, ps_tmp, ncols):
        for j in range(0, ncols, 512):
            w = min(512, ncols - j)
            self.MM(ps_tmp[:, 0:w], ones_1x128[0:1, :], row_ap[0:1, j:j + w])
            self.CPY(dst_sb[:, j:j + w], ps_tmp[:, 0:w])

    def transp_tile(self, dstT, src, iden, ps_tp, ncols=1024):
        for c in range(ncols // 128):
            self.TR(ps_tp[:, 0:128], src[:, c * 128:(c + 1) * 128], iden[:, :])
            self.CPY(dstT[:, c * 128:(c + 1) * 128], ps_tp[:, 0:128])


def build_ffn0():
    p = P("ffn0")
    x = p.din("x", [T, D])
    w1t = p.din("w1t", [D, 4096])
    w1n = p.din("w1n", [4096, D])
    w2t = p.din("w2t", [4096, D])
    w2n = p.din("w2n", [D, 4096])
    b1r = p.din("b1r", [128, 32])
    b2r = p.din("b2r", [128, 8])
    iden_in = p.din("iden", [128, 128])
    y = p.dout("y", [T, D])

    TT_ = 256
    NT = T // TT_

    sgn1 = p.sb("sgn1", [128, 8 * 4096], BF16)
    sgn2 = p.sb("sgn2", [128, 32 * 1024], BF16)
    h1T = p.sb("h1T", [128, 32 * TT_], BF16)
    hT = p.sb("hT", [128, 8 * TT_], BF16)
    xt = p.sb("xt", [128, 2 * 1024], F32)
    wtmp = p.sb("wtmp", [128, 4096], F32)
    scr = p.sb("scr", [128, 1024], F32)
    h = p.sb("h", [128, 1024], F32)
    o2T = p.sb("o2T", [128, TT_], F32)
    al1 = p.sb("al1", [128, 32], F32)
    al2 = p.sb("al2", [128, 8], F32)
    b1s = p.sb("b1s", [128, 32], F32)
    b2s = p.sb("b2s", [128, 8], F32)
    iden = p.sb("idn", [128, 128], F32)
    stat = p.sb("stat", [128, 4], F32)

    ps1 = p.ps("ps1", [128, 512])
    ps2 = p.ps("ps2", [128, 512])
    pst = p.ps("pst", [128, 128])

    p.DMA(iden[:, :], iden_in[:, :])
    p.DMA(b1s[:, :], b1r[:, :])
    p.DMA(b2s[:, :], b2r[:, :])

    for dc in range(8):
        p.DMA(wtmp[:, :], w1t[dc * 128:(dc + 1) * 128, :])
        p.ACT(sgn1[:, dc * 4096:(dc + 1) * 4096], wtmp[:, :], AF.Sign)
    for fb in range(32):
        p.DMA(wtmp[:, 0:1024], w1n[fb * 128:(fb + 1) * 128, :])
        p.RED(al1[:, fb:fb + 1], wtmp[:, 0:1024], ALU.add, absval=True)
    p.SMUL(al1[:, :], al1[:, :], 1.0 / D)
    for fc in range(32):
        p.DMA(wtmp[:, 0:1024], w2t[fc * 128:(fc + 1) * 128, :])
        p.ACT(sgn2[:, fc * 1024:(fc + 1) * 1024], wtmp[:, 0:1024], AF.Sign)
    for dc in range(8):
        p.DMA(wtmp[:, :], w2n[dc * 128:(dc + 1) * 128, :])
        p.RED(al2[:, dc:dc + 1], wtmp[:, :], ALU.add, absval=True)
    p.SMUL(al2[:, :], al2[:, :], 1.0 / 4096.0)

    for tt in range(NT):
        t0 = tt * TT_
        for ts in range(2):
            r0 = t0 + ts * 128
            p.DMA(xt[:, ts * 1024:(ts + 1) * 1024], x[r0:r0 + 128, :])
            xa = xt[:, ts * 1024:(ts + 1) * 1024]
            p.ln_plain(h[:, :], xa, scr[:, :], stat[:, 0:1], stat[:, 1:2], stat[:, 2:3])
            for c in range(8):
                p.TR(pst[:, 0:128], h[:, c * 128:(c + 1) * 128], iden[:, :])
                p.CPY(hT[:, c * TT_ + ts * 128: c * TT_ + ts * 128 + 128], pst[:, 0:128])
        for fb in range(32):
            for dc in range(8):
                p.MM(ps1[:, 0:TT_], sgn1[:, dc * 4096 + fb * 128: dc * 4096 + fb * 128 + 128],
                     hT[:, dc * TT_:(dc + 1) * TT_], start=(dc == 0), stop=(dc == 7))
            p.ACT(h1T[:, fb * TT_:(fb + 1) * TT_], ps1[:, 0:TT_], AF.Gelu,
                  bias=b1s[:, fb:fb + 1], scale=al1[:, fb:fb + 1])
        for dc in range(8):
            for fc in range(32):
                p.MM(ps2[:, 0:TT_], sgn2[:, fc * 1024 + dc * 128: fc * 1024 + dc * 128 + 128],
                     h1T[:, fc * TT_:(fc + 1) * TT_], start=(fc == 0), stop=(fc == 31))
            p.ACT(o2T[:, 0:TT_], ps2[:, 0:TT_], AF.Identity,
                  bias=b2s[:, dc:dc + 1], scale=al2[:, dc:dc + 1])
            for ts in range(2):
                p.TR(pst[:, 0:128], o2T[:, ts * 128:(ts + 1) * 128], iden[:, :])
                xa = xt[:, ts * 1024 + dc * 128: ts * 1024 + dc * 128 + 128]
                p.STT(xa, pst[:, 0:128], 0.5, xa, ALU.mult, ALU.add)
        for ts in range(2):
            r0 = t0 + ts * 128
            p.DMA(y[r0:r0 + 128, :], xt[:, ts * 1024:(ts + 1) * 1024])

    p.ser.build()
    return p.nc


def build_attn0():
    p = P("attn0")
    x = p.din("x", [T, D])
    wqt = p.din("wqt", [D, D])
    wkt = p.din("wkt", [D, D])
    wvt = p.din("wvt", [D, D])
    gsc_in = p.din("gsc", [1, 1])
    iden_in = p.din("iden", [128, 128])
    ones_in = p.din("ones1", [1, 128])
    y = p.dout("y", [T, D])
    import os
    dbg_out = p.dout("dbg", [8 * 16, 512]) if os.environ.get("ATTN0_DBG") else None
    dbg2 = p.dout("dbg2", [128, 64]) if os.environ.get("ATTN0_DBG") else None

    H = 16
    ST = 512
    NS = T // ST

    kT = p.sb("kT", [128, 8 * T], BF16)
    v65 = p.sb("v65", [128, TB * 16 * 65], BF16)
    qT = p.sb("qT", [128, 8 * ST], BF16)
    att = p.sb("att", [128, 4 * 1024], BF16)
    wbuf = p.sb("wbuf", [128, 8 * 1024], BF16)
    wld = p.sb("wld", [128, 1024], F32)
    xt = p.sb("xtl", [128, 1024], F32)
    h = p.sb("h", [128, 1024], F32)
    scr = p.sb("scr", [128, 1024], F32)
    qn = p.sb("qn", [128, 1024], F32)
    hT = p.sb("hT", [128, 8 * 128], BF16)
    esb = p.sb("esb", [128, ST], BF16)
    outT = p.sb("outT", [128, ST], F32)
    invd = p.sb("invd", [1, ST], F32)
    bcs = p.sb("bcs", [128, ST], F32)
    ssq = p.sb("ssq", [128, 16], F32)
    rn = p.sb("rn", [128, 16], F32)
    stat = p.sb("stat", [128, 4], F32)
    iden = p.sb("idn", [128, 128], F32)
    ones1 = p.sb("on1", [1, 128], F32)
    gsc = p.sb("gsc_sb", [128, 1], F32)

    psa = p.ps("psa", [128, 512])
    psb = p.ps("psb", [128, 512])
    psn = p.ps("psn", [128, 512])
    pst = p.ps("pst", [128, 128])

    p.DMA(iden[:, :], iden_in[:, :])
    p.DMA(ones1[:, :], ones_in[:, :])
    gs_row = p.sb("gsr", [1, 1], F32)
    ngsc = p.sb("ngsc", [128, 1], F32)
    p.DMA(gs_row[:, :], gsc_in[:, :])
    p.MM(pst[:, 0:1], ones1[0:1, :], gs_row[0:1, 0:1])
    p.CPY(gsc[:, :], pst[:, 0:1])
    p.ACT(ngsc[:, :], gsc[:, :], AF.Abs)
    p.SMUL(ngsc[:, :], ngsc[:, :], -1.0)

    p.MEMSET(v65[:, :], 1.0)

    def load_w(src):
        for dc in range(8):
            p.DMA(wld[:, :], src[dc * 128:(dc + 1) * 128, :])
            p.CPY(wbuf[:, dc * 1024:(dc + 1) * 1024], wld[:, :])

    def ln_and_hT(tb):
        p.DMA(xt[:, :], x[tb * 128:(tb + 1) * 128, :])
        p.ln_plain(h[:, :], xt[:, :], scr[:, :], stat[:, 0:1], stat[:, 1:2], stat[:, 2:3])
        for c in range(8):
            p.TR(pst[:, 0:128], h[:, c * 128:(c + 1) * 128], iden[:, :])
            p.CPY(hT[:, c * 128:(c + 1) * 128], pst[:, 0:128])

    def proj(dst_ps, et):
        for dc in range(8):
            p.MM(dst_ps[:, 0:512], hT[:, dc * 128:(dc + 1) * 128],
                 wbuf[:, dc * 1024 + et * 512: dc * 1024 + et * 512 + 512],
                 start=(dc == 0), stop=(dc == 7))

    def l2norm_to(dst, src_ps_list):
        for hh in range(16):
            et, off = divmod(hh * 64, 512)
            p.ACT(scr[:, 0:64], src_ps_list[et][:, off:off + 64], AF.Square,
                  accum=ssq[:, hh:hh + 1])
        p.ACT(rn[:, :], ssq[:, :], AF.Sqrt)
        p.TSS(rn[:, :], rn[:, :], 1e-12, ALU.max)
        p.RECIP(rn[:, :], rn[:, :])
        for hh in range(16):
            et, off = divmod(hh * 64, 512)
            p.TSS(dst[:, hh * 64:(hh + 1) * 64], src_ps_list[et][:, off:off + 64],
                  rn[:, hh:hh + 1], ALU.mult)

    load_w(wkt)
    for tb in range(TB):
        ln_and_hT(tb)
        proj(psa, 0)
        proj(psb, 1)
        l2norm_to(qn[:, :], [psa, psb])
        if dbg2 is not None and tb == 0:
            p.CPY(scr[:, 0:16], ssq[:, :])
            p.CPY(scr[:, 16:32], rn[:, :])
            p.CPY(scr[:, 32:48], qn[:, 0:16])
            p.DMA(dbg2[:, 0:48], scr[:, 0:48])
        for c in range(8):
            p.TR(pst[:, 0:128], qn[:, c * 128:(c + 1) * 128], iden[:, :])
            p.CPY(kT[:, c * T + tb * 128: c * T + tb * 128 + 128], pst[:, 0:128])

    load_w(wvt)
    for tb in range(TB):
        ln_and_hT(tb)
        proj(psa, 0)
        proj(psb, 1)
        for hh in range(16):
            et, off = divmod(hh * 64, 512)
            src = psa if et == 0 else psb
            p.CPY(v65[:, tb * 1040 + hh * 65: tb * 1040 + hh * 65 + 64],
                  src[:, off:off + 64])

    load_w(wqt)
    for st in range(NS):
        for ts in range(4):
            tb = st * 4 + ts
            ln_and_hT(tb)
            proj(psa, 0)
            proj(psb, 1)
            l2norm_to(qn[:, :], [psa, psb])
            for c in range(8):
                p.TR(pst[:, 0:128], qn[:, c * 128:(c + 1) * 128], iden[:, :])
                p.CPY(qT[:, c * ST + ts * 128: c * ST + ts * 128 + 128], pst[:, 0:128])
        for hh in range(16):
            ec, poff = hh // 2, (hh % 2) * 64
            for kb in range(TB):
                p.MM(psa[:, 0:ST],
                     kT[poff:poff + 64, ec * T + kb * 128: ec * T + kb * 128 + 128],
                     qT[poff:poff + 64, ec * ST: ec * ST + ST])
                p.ACT(esb[:, :], psa[:, 0:ST], AF.Exp, scale=gsc[:, 0:1],
                      bias=ngsc[:, 0:1])
                p.MM(psn[0:65, 0:ST], v65[:, kb * 1040 + hh * 65: kb * 1040 + hh * 65 + 65],
                     esb[:, :], start=(kb == 0), stop=(kb == TB - 1))
            p.RECIP(invd[:, :], psn[64:65, 0:ST])
            if dbg_out is not None:
                p.CPY(bcs[0:1, :], psn[64:65, 0:ST])
                p.DMA(dbg_out[st * 16 + hh: st * 16 + hh + 1, :], bcs[0:1, :])
            p.MM(psb[0:64, 0:ST], ones1[0:1, 0:64], invd[0:1, 0:ST])
            p.CPY(bcs[0:64, :], psb[0:64, 0:ST])
            p.TT(ALU.mult, outT[0:64, :], psn[0:64, 0:ST], bcs[0:64, :])
            for qs in range(4):
                p.TR(pst[:, 0:64], outT[0:64, qs * 128:(qs + 1) * 128], iden[0:64, 0:64])
                p.CPY(att[:, qs * 1024 + hh * 64: qs * 1024 + hh * 64 + 64], pst[:, 0:64])
        for qs in range(4):
            tb = st * 4 + qs
            p.DMA(xt[:, :], x[tb * 128:(tb + 1) * 128, :])
            p.TT(ALU.add, xt[:, :], xt[:, :], att[:, qs * 1024:(qs + 1) * 1024])
            p.DMA(y[tb * 128:(tb + 1) * 128, :], xt[:, :])

    p.ser.build()
    return p.nc


def build_scores():
    p = P("scores")
    x = p.din("x", [T, D])
    gwt = p.din("gwt", [D, 8])
    iden_in = p.din("iden", [128, 128])
    s_out = p.dout("s", [T, 8])

    xt = p.sb("xt", [128, 1024], F32)
    h = p.sb("h", [128, 1024], F32)
    scr = p.sb("scr", [128, 1024], F32)
    hTf = p.sb("hTf", [128, 8 * 128], F32)
    gw = p.sb("gw", [128, 64], F32)
    st = p.sb("st", [128, 8], F32)
    stat = p.sb("stat", [128, 4], F32)
    iden = p.sb("idn", [128, 128], F32)
    psa = p.ps("psa", [128, 128])
    pst = p.ps("pst", [128, 128])

    p.DMA(iden[:, :], iden_in[:, :])
    for dc in range(8):
        p.DMA(gw[:, dc * 8:(dc + 1) * 8], gwt[dc * 128:(dc + 1) * 128, :])

    for tb in range(TB):
        p.DMA(xt[:, :], x[tb * 128:(tb + 1) * 128, :])
        p.ln_plain(h[:, :], xt[:, :], scr[:, :], stat[:, 0:1], stat[:, 1:2], stat[:, 2:3])
        for c in range(8):
            p.TR(pst[:, 0:128], h[:, c * 128:(c + 1) * 128], iden[:, :])
            p.CPY(hTf[:, c * 128:(c + 1) * 128], pst[:, 0:128])
        for dc in range(8):
            p.MM(psa[:, 0:8], hTf[:, dc * 128:(dc + 1) * 128], gw[:, dc * 8:(dc + 1) * 8],
                 start=(dc == 0), stop=(dc == 7))
        p.ACT(st[:, :], psa[:, 0:8], AF.Tanh)
        p.TS2(st[:, :], st[:, :], 1.0, 0.5, ALU.add, ALU.mult)
        p.DMA(s_out[tb * 128:(tb + 1) * 128, :], st[:, :])

    p.ser.build()
    return p.nc


def build_moe():
    p = P("moe")
    x = p.din("x", [T, D])
    sc = p.din("sc", [128, 8 * 256])
    sm = p.din("sm", [128, 8 * 32])
    fc1t = p.din("fc1t", [8, D, 4096])
    fc2t = p.din("fc2t", [8, 4096, D])
    b1sr = p.din("b1sr", [128, 8 * 32])
    b2sr = p.din("b2sr", [1, 8 * 1024])
    gam = p.din("gam", [1, D])
    bet = p.din("bet", [1, D])
    iden_in = p.din("iden", [128, 128])
    ones_in = p.din("ones1", [1, 128])
    y = p.dout("y", [T, D])

    TT_ = 512
    NT = T // TT_

    w1e = p.sb("w1e", [128, 8 * 4096], BF16)
    h1T = p.sb("h1T", [128, 32 * TT_], BF16)
    yT = p.sb("yT", [128, 8 * TT_], BF16)
    w2c = p.sb("w2c", [128, 1024], BF16)
    wld = p.sb("wld", [128, 4096], F32)
    xt = p.sb("xt", [128, 4 * 1024], F32)
    scs = p.sb("scs", [128, 8 * 256], F32)
    sms = p.sb("sms", [128, 8 * 32], F32)
    wsel = p.sb("wsel", [128, 8 * 32], F32)
    h = p.sb("h", [128, 1024], F32)
    yn = p.sb("yn", [128, 1024], F32)
    scr = p.sb("scr", [128, 1024], F32)
    gb = p.sb("gb", [128, 1024], F32)
    bb = p.sb("bb", [128, 1024], F32)
    b2e = p.sb("b2e", [128, 1024], F32)
    b2row = p.sb("b2row", [1, 1024], F32)
    b1s = p.sb("b1s", [128, 8 * 32], F32)
    th = p.sb("th", [128, 8], F32)
    cmpt = p.sb("cmpt", [128, 256], F32)
    pcol = p.sb("pcol", [128, 1], F32)
    d1 = p.sb("d1", [128, 1], F32)
    stat = p.sb("stat", [128, 4], F32)
    iden = p.sb("idn", [128, 128], F32)
    ones1 = p.sb("on1", [1, 128], F32)
    onescol = p.sb("onc", [128, 1], F32)

    psA = [p.ps("psA%d" % i, [128, 1024]) for i in range(4)]
    ps1 = psA[0]
    pst = psA[1]
    psc = psA[2]
    cnt_sb = p.sb("cnt_sb", [1, 1], F32)

    p.DMA(iden[:, :], iden_in[:, :])
    p.DMA(ones1[:, :], ones_in[:, :])
    p.DMA(scs[:, :], sc[:, :])
    p.DMA(sms[:, :], sm[:, :])
    p.DMA(b1s[:, :], b1sr[:, :])
    gr = p.sb("gr", [1, 1024], F32)
    p.DMA(gr[:, :], gam[:, :])
    p.bcast_row(gb, gr, ones1, ps1, 1024)
    p.DMA(gr[:, :], bet[:, :])
    p.bcast_row(bb, gr, ones1, ps1, 1024)
    p.MEMSET(onescol[:, :], 1.0)

    K_TOP = 4096.0
    p.MEMSET(th[:, :], 0.5)
    for e in range(8):
        delta = 0.25
        for _ in range(36):
            p.TSS(cmpt[:, :], scs[:, e * 256:(e + 1) * 256], th[:, e:e + 1], ALU.is_gt)
            p.RED(pcol[:, :], cmpt[:, :], ALU.add)
            p.MM(psc[0:1, 0:1], pcol[:, 0:1], onescol[:, 0:1])
            p.CPY(cnt_sb[0:1, 0:1], psc[0:1, 0:1])
            p.MM(pst[:, 0:1], ones1[0:1, :], cnt_sb[0:1, 0:1])
            p.TSS(d1[:, :], pst[:, 0:1], K_TOP, ALU.is_gt)
            p.TS2(d1[:, :], d1[:, :], 2.0 * delta, -delta, ALU.mult, ALU.add)
            p.TT(ALU.add, th[:, e:e + 1], th[:, e:e + 1], d1[:, :])
            delta *= 0.5
        p.TSS(th[:, e:e + 1], th[:, e:e + 1], 2.0 * delta, ALU.add)
        p.TSS(wsel[:, e * 32:(e + 1) * 32], sms[:, e * 32:(e + 1) * 32],
              th[:, e:e + 1], ALU.is_gt)
        p.TT(ALU.mult, wsel[:, e * 32:(e + 1) * 32], wsel[:, e * 32:(e + 1) * 32],
             sms[:, e * 32:(e + 1) * 32])
    p.TSS(wsel[:, :], wsel[:, :], 0.5, ALU.mult)

    for tt in range(NT):
        t0 = tt * TT_
        for ts in range(4):
            r0 = t0 + ts * 128
            xa = xt[:, ts * 1024:(ts + 1) * 1024]
            p.DMA(xa, x[r0:r0 + 128, :])
            p.ln_plain(h[:, :], xa, scr[:, :], stat[:, 0:1], stat[:, 1:2], stat[:, 2:3])
            p.STT(xa, h[:, :], 0.5, xa, ALU.mult, ALU.add)
            p.ln_plain(yn[:, :], h[:, :], scr[:, :], stat[:, 0:1], stat[:, 1:2], stat[:, 2:3])
            p.TT(ALU.mult, yn[:, :], yn[:, :], gb[:, :])
            p.TT(ALU.add, yn[:, :], yn[:, :], bb[:, :])
            for c in range(8):
                p.TR(pst[:, 0:128], yn[:, c * 128:(c + 1) * 128], iden[:, :])
                p.CPY(yT[:, c * TT_ + ts * 128: c * TT_ + ts * 128 + 128], pst[:, 0:128])
        for e in range(8):
            for dc in range(8):
                p.DMA(wld[:, :], fc1t[e, dc * 128:(dc + 1) * 128, :])
                p.CPY(w1e[:, dc * 4096:(dc + 1) * 4096], wld[:, :])
            p.DMA(b2row[:, :], b2sr[0:1, e * 1024:(e + 1) * 1024])
            p.bcast_row(b2e, b2row, ones1, ps1, 1024)
            for fb in range(32):
                for dc in range(8):
                    p.MM(ps1[:, 0:TT_],
                         w1e[:, dc * 4096 + fb * 128: dc * 4096 + fb * 128 + 128],
                         yT[:, dc * TT_:(dc + 1) * TT_], start=(dc == 0), stop=(dc == 7))
                p.ACT(h1T[:, fb * TT_:(fb + 1) * TT_], ps1[:, 0:TT_], AF.Gelu_apprx_tanh,
                      bias=b1s[:, e * 32 + fb: e * 32 + fb + 1])
            for fc in range(32):
                p.DMA(wld[:, 0:1024], fc2t[e, fc * 128:(fc + 1) * 128, :])
                p.CPY(w2c[:, :], wld[:, 0:1024])
                for ts in range(4):
                    for dt in range(2):
                        p.MM(psA[ts][:, dt * 512:(dt + 1) * 512],
                             h1T[:, fc * TT_ + ts * 128: fc * TT_ + ts * 128 + 128],
                             w2c[:, dt * 512:(dt + 1) * 512],
                             start=(fc == 0), stop=(fc == 31))
            for ts in range(4):
                for dt in range(2):
                    xa = xt[:, ts * 1024 + dt * 512: ts * 1024 + dt * 512 + 512]
                    wap = wsel[:, e * 32 + tt * 4 + ts: e * 32 + tt * 4 + ts + 1]
                    p.TT(ALU.add, scr[:, 0:512], psA[ts][:, dt * 512:(dt + 1) * 512],
                         b2e[:, dt * 512:(dt + 1) * 512])
                    p.STT(xa, scr[:, 0:512], wap, xa, ALU.mult, ALU.add)
        for ts in range(4):
            r0 = t0 + ts * 128
            p.DMA(y[r0:r0 + 128, :], xt[:, ts * 1024:(ts + 1) * 1024])

    p.ser.build()
    return p.nc


def build_l1a():
    p = P("l1a")
    x = p.din("x", [T, D])
    ln1g = p.din("ln1g", [1, D]); ln1b = p.din("ln1b", [1, D])
    wq = p.din("wq", [D, D]); wk = p.din("wk", [D, D])
    wv = p.din("wv", [D, D]); wg = p.din("wg", [D, D])
    wo = p.din("wo", [D, D])
    bq = p.din("bq", [1, D]); bk = p.din("bk", [1, D]); bv = p.din("bv", [1, D])
    bo = p.din("bo", [1, D]); bg = p.din("bg", [1, D])
    a_in = p.din("a", [64, 1])
    bw = p.din("bw", [D, 64])
    cw = p.din("cw", [64, D])
    mdiag = p.din("mdiag", [128, 128])
    mprev = p.din("mprev", [128, 128])
    iden_in = p.din("iden", [128, 128])
    ones_in = p.din("ones1", [1, 128])
    x1o = p.dout("x1", [T, D])

    H = 16

    hT_dram = p.dtmp("hTd", [TB, 128, 1024], BF16)

    wbuf4 = p.sb("wbuf4", [128, 4 * 8 * 1024], BF16)
    wobuf = p.sb("wobuf", [128, 8 * 1024], BF16)
    bwb = p.sb("bwb", [128, 8 * 64], BF16)
    cwb = p.sb("cwb", [128, 1024], BF16)
    wld = p.sb("wld", [128, 1024], F32)
    xt = p.sb("xt", [128, 1024], F32)
    xin = p.sb("xin", [128, 1024], F32)
    h = p.sb("h", [128, 1024], F32)
    scr = p.sb("scr", [128, 1024], F32)
    scr2 = p.sb("scr2", [128, 1024], F32)
    hT = p.sb("hT", [128, 8 * 128], BF16)
    u = p.sb("u", [128, T], BF16)
    stf = p.sb("stf", [128, T], F32)
    sttile = p.sb("sttile", [128, 128], BF16)
    abc = p.sb("abc", [128, 1024], F32)
    qTr = [p.sb("qTr%d" % i, [128, 8 * 128], BF16) for i in range(2)]
    kTr = [p.sb("kTr%d" % i, [128, 8 * 128], BF16) for i in range(2)]
    v65r = [p.sb("v65r%d" % i, [128, 16 * 65], BF16) for i in range(2)]
    aoT = p.sb("aoT", [128, 8 * 128], BF16)
    esb = p.sb("esb", [128, 128], F32)
    ebf = p.sb("ebf", [128, 128], BF16)
    invd = p.sb("invd", [1, 128], F32)
    bcs = p.sb("bcs", [128, 128], F32)
    g_sb = p.sb("g_sb", [128, 1024], F32)
    attn = p.sb("attn", [128, 1024], F32)
    bqb = p.sb("bqb", [128, 1024], F32); bkb = p.sb("bkb", [128, 1024], F32)
    bvb = p.sb("bvb", [128, 1024], F32); bob = p.sb("bob", [128, 1024], F32)
    bgb = p.sb("bgb", [128, 1024], F32)
    g1b = p.sb("g1b", [128, 1024], F32); b1b = p.sb("b1b", [128, 1024], F32)
    msd = p.sb("msd", [128, 128], F32)
    msp = p.sb("msp", [128, 128], F32)
    stat = p.sb("stat", [128, 4], F32)
    iden = p.sb("idn", [128, 128], F32)
    ones1 = p.sb("on1", [1, 128], F32)
    acol = p.sb("acol", [128, 1], F32)

    psa = p.ps("psa", [128, 512])
    psb = p.ps("psb", [128, 512])
    psn = p.ps("psn", [128, 128])
    psq = p.ps("psq", [128, 128])
    pst = p.ps("pst", [128, 128])

    p.DMA(iden[:, :], iden_in[:, :])
    p.DMA(ones1[:, :], ones_in[:, :])
    p.DMA(msd[:, :], mdiag[:, :])
    p.DMA(msp[:, :], mprev[:, :])
    p.DMA(acol[0:64, 0:1], a_in[:, :])
    rowst = p.sb("rowst", [1, 1024], F32)
    for row, dst in ((bq, bqb), (bk, bkb), (bv, bvb), (bo, bob), (bg, bgb),
                     (ln1g, g1b), (ln1b, b1b)):
        p.DMA(rowst[:, :], row[:, :])
        p.bcast_row(dst, rowst, ones1, psa, 1024)
    for wi, src in enumerate((wq, wk, wv, wg)):
        for dc in range(8):
            p.DMA(wld[:, :], src[dc * 128:(dc + 1) * 128, :])
            p.CPY(wbuf4[:, wi * 8192 + dc * 1024: wi * 8192 + (dc + 1) * 1024], wld[:, :])
    for dc in range(8):
        p.DMA(wld[:, :], wo[dc * 128:(dc + 1) * 128, :])
        p.CPY(wobuf[:, dc * 1024:(dc + 1) * 1024], wld[:, :])
        p.DMA(wld[:, 0:64], bw[dc * 128:(dc + 1) * 128, :])
        p.CPY(bwb[:, dc * 64:(dc + 1) * 64], wld[:, 0:64])
    p.DMA(wld[0:64, :], cw[:, :])
    p.CPY(cwb[0:64, :], wld[0:64, :])
    for r in (v65r[0], v65r[1]):
        p.MEMSET(r[:, :], 1.0)

    def ln_both(tb):
        p.DMA(xt[:, :], x[tb * 128:(tb + 1) * 128, :])
        p.ln_plain(xin[:, :], xt[:, :], scr[:, :], stat[:, 0:1], stat[:, 1:2], stat[:, 2:3])
        p.ln_plain(h[:, :], xin[:, :], scr[:, :], stat[:, 0:1], stat[:, 1:2], stat[:, 2:3])
        p.TT(ALU.mult, h[:, :], h[:, :], g1b[:, :])
        p.TT(ALU.add, h[:, :], h[:, :], b1b[:, :])

    for tb in range(TB):
        ln_both(tb)
        for c in range(8):
            p.TR(pst[:, 0:128], h[:, c * 128:(c + 1) * 128], iden[:, :])
            p.CPY(hT[:, c * 128:(c + 1) * 128], pst[:, 0:128])
        p.DMA(hT_dram[tb, :, :], hT[:, :])
        for dc in range(8):
            p.MM(psn[0:64, 0:128], bwb[:, dc * 64:(dc + 1) * 64],
                 hT[:, dc * 128:(dc + 1) * 128], start=(dc == 0), stop=(dc == 7))
        p.CPY(u[0:64, tb * 128:(tb + 1) * 128], psn[0:64, 0:128])

    p.MEMSET(abc[0:64, :], 1.0)
    p.TSS(abc[0:64, :], abc[0:64, :], acol[0:64, 0:1], ALU.mult)
    for cch in range(4):
        init = 0.0 if cch == 0 else stf[0:64, cch * 1024 - 1: cch * 1024]
        p.SCAN(stf[0:64, cch * 1024:(cch + 1) * 1024], abc[0:64, 0:1024],
               u[0:64, cch * 1024:(cch + 1) * 1024], init, ALU.mult, ALU.add)

    for tb in range(TB):
        cur, prv = tb % 2, (tb + 1) % 2
        p.DMA(hT[:, :], hT_dram[tb, :, :])
        for wi, dstT in ((0, qTr[cur]), (1, kTr[cur])):
            for et in range(2):
                pp = psa if et == 0 else psb
                for dc in range(8):
                    p.MM(pp[:, 0:512], hT[:, dc * 128:(dc + 1) * 128],
                         wbuf4[:, wi * 8192 + dc * 1024 + et * 512:
                               wi * 8192 + dc * 1024 + et * 512 + 512],
                         start=(dc == 0), stop=(dc == 7))
            badd = bqb if wi == 0 else bkb
            p.TT(ALU.add, scr[:, 0:512], psa[:, 0:512], badd[:, 0:512])
            p.TT(ALU.add, scr[:, 512:1024], psb[:, 0:512], badd[:, 512:1024])
            for c in range(8):
                p.TR(pst[:, 0:128], scr[:, c * 128:(c + 1) * 128], iden[:, :])
                p.CPY(dstT[:, c * 128:(c + 1) * 128], pst[:, 0:128])
        for et in range(2):
            pp = psa if et == 0 else psb
            for dc in range(8):
                p.MM(pp[:, 0:512], hT[:, dc * 128:(dc + 1) * 128],
                     wbuf4[:, 2 * 8192 + dc * 1024 + et * 512:
                           2 * 8192 + dc * 1024 + et * 512 + 512],
                     start=(dc == 0), stop=(dc == 7))
        p.TT(ALU.add, scr[:, 0:512], psa[:, 0:512], bvb[:, 0:512])
        p.TT(ALU.add, scr[:, 512:1024], psb[:, 0:512], bvb[:, 512:1024])
        for hh in range(16):
            p.CPY(v65r[cur][:, hh * 65: hh * 65 + 64], scr[:, hh * 64:(hh + 1) * 64])
        for et in range(2):
            pp = psa if et == 0 else psb
            for dc in range(8):
                p.MM(pp[:, 0:512], hT[:, dc * 128:(dc + 1) * 128],
                     wbuf4[:, 3 * 8192 + dc * 1024 + et * 512:
                           3 * 8192 + dc * 1024 + et * 512 + 512],
                     start=(dc == 0), stop=(dc == 7))
            p.TT(ALU.add, scr2[:, et * 512:(et + 1) * 512], pp[:, 0:512],
                 bgb[:, et * 512:(et + 1) * 512])
        p.ACT(g_sb[:, :], scr2[:, 0:1024], AF.Sigmoid)
        for hh in range(16):
            ec, poff = hh // 2, (hh % 2) * 64
            first = True
            if tb > 0:
                p.MM(psq[:, 0:128],
                     kTr[prv][poff:poff + 64, ec * 128:(ec + 1) * 128],
                     qTr[cur][poff:poff + 64, ec * 128:(ec + 1) * 128])
                p.ACT(esb[:, :], psq[:, 0:128], AF.Exp, scale=0.125)
                p.TT(ALU.mult, ebf[:, :], esb[:, :], msp[:, :])
                p.MM(psn[0:65, 0:128], v65r[prv][:, hh * 65: hh * 65 + 65], ebf[:, :],
                     start=True, stop=False)
                first = False
            p.MM(psq[:, 0:128],
                 kTr[cur][poff:poff + 64, ec * 128:(ec + 1) * 128],
                 qTr[cur][poff:poff + 64, ec * 128:(ec + 1) * 128])
            p.ACT(esb[:, :], psq[:, 0:128], AF.Exp, scale=0.125)
            p.TT(ALU.mult, ebf[:, :], esb[:, :], msd[:, :])
            p.MM(psn[0:65, 0:128], v65r[cur][:, hh * 65: hh * 65 + 65], ebf[:, :],
                 start=first, stop=True)
            p.RECIP(invd[:, :], psn[64:65, 0:128])
            p.MM(psq[0:64, 0:128], ones1[0:1, 0:64], invd[0:1, :])
            p.CPY(bcs[0:64, :], psq[0:64, 0:128])
            p.TT(ALU.mult, aoT[poff:poff + 64, ec * 128:(ec + 1) * 128],
                 psn[0:64, 0:128], bcs[0:64, :])
        for dt in range(2):
            pp = psa if dt == 0 else psb
            for ec in range(8):
                p.MM(pp[:, 0:512], aoT[:, ec * 128:(ec + 1) * 128],
                     wobuf[:, ec * 1024 + dt * 512: ec * 1024 + dt * 512 + 512],
                     start=(ec == 0), stop=(ec == 7))
            p.TT(ALU.add, attn[:, dt * 512:(dt + 1) * 512], pp[:, 0:512],
                 bob[:, dt * 512:(dt + 1) * 512])
        p.CPY(sttile[0:64, 0:128], stf[0:64, tb * 128:(tb + 1) * 128])
        for dt in range(2):
            pp = psa if dt == 0 else psb
            p.MM(pp[:, 0:512], sttile[0:64, 0:128],
                 cwb[0:64, dt * 512:(dt + 1) * 512])
        ln_both(tb)
        for dt in range(2):
            pp = psa if dt == 0 else psb
            sl = slice(dt * 512, (dt + 1) * 512)
            p.TT(ALU.subtract, scr[:, sl], attn[:, sl], pp[:, 0:512])
            p.TT(ALU.mult, scr[:, sl], scr[:, sl], g_sb[:, sl])
            p.TT(ALU.add, scr2[:, sl], xin[:, sl], pp[:, 0:512])
            p.TT(ALU.add, scr2[:, sl], scr2[:, sl], scr[:, sl])
        p.DMA(x1o[tb * 128:(tb + 1) * 128, :], scr2[:, 0:1024])

    p.ser.build()
    return p.nc


def build_l1b():
    p = P("l1b")
    x = p.din("x", [T, D])
    x1 = p.din("x1", [T, D])
    ln2g = p.din("ln2g", [1, D]); ln2b = p.din("ln2b", [1, D])
    w1 = p.din("w1", [D, 4096])
    b1r = p.din("b1r", [128, 32])
    w2 = p.din("w2", [4096, D])
    b2row = p.din("b2row", [1, D])
    iden_in = p.din("iden", [128, 128])
    ones_in = p.din("ones1", [1, 128])
    y = p.dout("y", [T, D])

    TT_ = 256
    NT = T // TT_

    w1b = p.sb("w1b", [128, 8 * 4096], BF16)
    w2b = p.sb("w2b", [128, 32 * 1024], BF16)
    h1T = p.sb("h1T", [128, 32 * TT_], BF16)
    h2T = p.sb("h2T", [128, 8 * TT_], BF16)
    wld = p.sb("wld", [128, 1024], F32)
    xt = p.sb("xt", [128, 2 * 1024], F32)
    x1t = p.sb("x1t", [128, 2 * 1024], F32)
    h = p.sb("h", [128, 1024], F32)
    scr = p.sb("scr", [128, 1024], F32)
    g2b = p.sb("g2b", [128, 1024], F32)
    b2b = p.sb("b2b", [128, 1024], F32)
    b1s = p.sb("b1s", [128, 32], F32)
    stat = p.sb("stat", [128, 4], F32)
    iden = p.sb("idn", [128, 128], F32)
    ones1 = p.sb("on1", [1, 128], F32)

    ps1 = p.ps("ps1", [128, 512])
    ps2 = p.ps("ps2", [128, 512])
    pst = p.ps("pst", [128, 128])

    p.DMA(iden[:, :], iden_in[:, :])
    p.DMA(ones1[:, :], ones_in[:, :])
    p.DMA(b1s[:, :], b1r[:, :])
    rowst = p.sb("rowst", [1, 1024], F32)
    b2bc = p.sb("b2bc", [128, 1024], F32)
    for row, dst in ((ln2g, g2b), (ln2b, b2b), (b2row, b2bc)):
        p.DMA(rowst[:, :], row[:, :])
        p.bcast_row(dst, rowst, ones1, ps1, 1024)
    for dc in range(8):
        for q4 in range(4):
            p.DMA(wld[:, :], w1[dc * 128:(dc + 1) * 128, q4 * 1024:(q4 + 1) * 1024])
            p.CPY(w1b[:, dc * 4096 + q4 * 1024: dc * 4096 + (q4 + 1) * 1024], wld[:, :])
    for fc in range(32):
        p.DMA(wld[:, 0:1024], w2[fc * 128:(fc + 1) * 128, :])
        p.CPY(w2b[:, fc * 1024:(fc + 1) * 1024], wld[:, 0:1024])

    for tt in range(NT):
        t0 = tt * TT_
        for ts in range(2):
            r0 = t0 + ts * 128
            p.DMA(x1t[:, ts * 1024:(ts + 1) * 1024], x1[r0:r0 + 128, :])
            xa = x1t[:, ts * 1024:(ts + 1) * 1024]
            p.ln_plain(h[:, :], xa, scr[:, :], stat[:, 0:1], stat[:, 1:2], stat[:, 2:3])
            p.TT(ALU.mult, h[:, :], h[:, :], g2b[:, :])
            p.TT(ALU.add, h[:, :], h[:, :], b2b[:, :])
            for c in range(8):
                p.TR(pst[:, 0:128], h[:, c * 128:(c + 1) * 128], iden[:, :])
                p.CPY(h2T[:, c * TT_ + ts * 128: c * TT_ + ts * 128 + 128], pst[:, 0:128])
        for fb in range(32):
            for dc in range(8):
                p.MM(ps1[:, 0:TT_], w1b[:, dc * 4096 + fb * 128: dc * 4096 + fb * 128 + 128],
                     h2T[:, dc * TT_:(dc + 1) * TT_], start=(dc == 0), stop=(dc == 7))
            p.ACT(h1T[:, fb * TT_:(fb + 1) * TT_], ps1[:, 0:TT_], AF.Gelu,
                  bias=b1s[:, fb:fb + 1])
        for ts in range(2):
            r0 = t0 + ts * 128
            p.DMA(xt[:, ts * 1024:(ts + 1) * 1024], x[r0:r0 + 128, :])
            for dt in range(2):
                for fc in range(32):
                    p.MM(ps2[:, 0:512],
                         h1T[:, fc * TT_ + ts * 128: fc * TT_ + ts * 128 + 128],
                         w2b[:, fc * 1024 + dt * 512: fc * 1024 + dt * 512 + 512],
                         start=(fc == 0), stop=(fc == 31))
                sl = slice(dt * 512, (dt + 1) * 512)
                xa = xt[:, ts * 1024 + dt * 512: ts * 1024 + dt * 512 + 512]
                x1a = x1t[:, ts * 1024 + dt * 512: ts * 1024 + dt * 512 + 512]
                p.TT(ALU.add, scr[:, 0:512], ps2[:, 0:512], b2bc[:, sl])
                p.TT(ALU.add, scr[:, 0:512], scr[:, 0:512], x1a)
                p.TT(ALU.add, xa, xa, scr[:, 0:512])
            p.DMA(y[r0:r0 + 128, :], xt[:, ts * 1024:(ts + 1) * 1024])

    p.ser.build()
    return p.nc


_PROGS = {}


def _get(name, builder):
    if name not in _PROGS:
        _PROGS[name] = builder()
    return _PROGS[name]


def _run(nc, in_maps, outkey):
    res = run_bass_kernel_spmd(nc, in_maps, core_ids=list(range(NCORE)))
    return [np.asarray(r[outkey]) for r in res.results]


def kernel(**inp):
    f32 = np.float32
    C = lambda a: np.ascontiguousarray(np.asarray(a, dtype=f32))
    iden = np.eye(128, dtype=f32)
    ones1 = np.ones((1, 128), dtype=f32)

    x = C(inp["L0_attn_x"])
    xs = [x[c] for c in range(NCORE)]

    w1 = C(inp["L0_ffn_w1"]); w2 = C(inp["L0_ffn_w2"])
    ffn0_static = {
        "w1t": C(w1.T), "w1n": w1, "w2t": C(w2.T), "w2n": w2,
        "b1r": C(inp["L0_ffn_b1"].reshape(32, 128).T),
        "b2r": C(inp["L0_ffn_b2"].reshape(8, 128).T),
        "iden": iden,
    }
    nc_ffn0 = _get("ffn0", build_ffn0)
    xs = _run(nc_ffn0, [dict(x=xc, **ffn0_static) for xc in xs], "y")

    attn_static = {
        "wqt": C(inp["L0_attn_Wq"].T), "wkt": C(inp["L0_attn_Wk"].T),
        "wvt": C(inp["L0_attn_Wv"].T),
        "gsc": C(inp["L0_attn_g_scale"]).reshape(1, 1),
        "iden": iden, "ones1": ones1,
    }
    nc_attn0 = _get("attn0", build_attn0)
    xs = _run(nc_attn0, [dict(x=xc, **attn_static) for xc in xs], "y")

    xs = _run(nc_ffn0, [dict(x=xc, **ffn0_static) for xc in xs], "y")

    nc_scores = _get("scores", build_scores)
    nc_moe = _get("moe", build_moe)
    gwt = C(inp["L1_ffn_gate_w"].T)
    b1sr = np.concatenate(
        [C(inp["L1_ffn_b1s"][e].reshape(32, 128).T) for e in range(8)], axis=1)
    moe_static = {
        "fc1t": C(np.transpose(inp["L1_ffn_fc1s"], (0, 2, 1))),
        "fc2t": C(np.transpose(inp["L1_ffn_fc2s"], (0, 2, 1))),
        "b1sr": C(b1sr),
        "b2sr": C(inp["L1_ffn_b2s"].reshape(1, 8 * 1024)),
        "gam": C(inp["L1_ffn_ln_gamma"]).reshape(1, D),
        "bet": C(inp["L1_ffn_ln_beta"]).reshape(1, D),
        "iden": iden, "ones1": ones1,
    }

    def run_moe(xs):
        ss = _run(nc_scores, [dict(x=xc, gwt=gwt, iden=iden) for xc in xs], "s")
        s_all = np.concatenate(ss, axis=0)
        sc = np.concatenate(
            [C(s_all[:, e].reshape(256, 128).T) for e in range(8)], axis=1)
        maps = []
        for c in range(NCORE):
            sm = np.concatenate(
                [C(ss[c][:, e].reshape(32, 128).T) for e in range(8)], axis=1)
            maps.append(dict(x=xs[c], sc=sc, sm=C(sm), **moe_static))
        return _run(nc_moe, maps, "y")

    xs = run_moe(xs)

    tri = np.arange(128)
    mdiag = (tri[None, :] >= tri[:, None]).astype(f32)
    mprev = (tri[:, None] > tri[None, :]).astype(f32)
    l1a_static = {
        "ln1g": C(inp["L1_attn_ln1_g"]).reshape(1, D),
        "ln1b": C(inp["L1_attn_ln1_b"]).reshape(1, D),
        "wq": C(inp["L1_attn_WQ"]), "wk": C(inp["L1_attn_WK"]),
        "wv": C(inp["L1_attn_WV"]), "wg": C(inp["L1_attn_Wg"]),
        "wo": C(inp["L1_attn_WO"]),
        "bq": C(inp["L1_attn_bQ"]).reshape(1, D),
        "bk": C(inp["L1_attn_bK"]).reshape(1, D),
        "bv": C(inp["L1_attn_bV"]).reshape(1, D),
        "bo": C(inp["L1_attn_bO"]).reshape(1, D),
        "bg": C(inp["L1_attn_bg"]).reshape(1, D),
        "a": C(inp["L1_attn_A"]).reshape(64, 1),
        "bw": C(inp["L1_attn_Bw"]), "cw": C(inp["L1_attn_Cw"]),
        "mdiag": mdiag, "mprev": mprev, "iden": iden, "ones1": ones1,
    }
    nc_l1a = _get("l1a", build_l1a)
    x_in = xs
    x1s = _run(nc_l1a, [dict(x=xc, **l1a_static) for xc in x_in], "x1")

    l1b_static = {
        "ln2g": C(inp["L1_attn_ln2_g"]).reshape(1, D),
        "ln2b": C(inp["L1_attn_ln2_b"]).reshape(1, D),
        "w1": C(inp["L1_attn_W1"]),
        "b1r": C(inp["L1_attn_b1"].reshape(32, 128).T),
        "w2": C(inp["L1_attn_W2"]),
        "b2row": C(inp["L1_attn_b2"]).reshape(1, D),
        "iden": iden, "ones1": ones1,
    }
    nc_l1b = _get("l1b", build_l1b)
    xs = _run(nc_l1b, [dict(x=x_in[c], x1=x1s[c], **l1b_static)
                       for c in range(NCORE)], "y")

    xs = run_moe(xs)

    return np.stack(xs, axis=0)
